# version 01
# baseline (speedup 1.0000x reference)
import numpy as np
import concourse.bass as bass
import concourse.mybir as mybir
from concourse.bass_utils import run_bass_kernel_spmd
from contextlib import ExitStack

F32 = mybir.dt.float32
BF16 = mybir.dt.bfloat16
I32 = mybir.dt.int32
AF = mybir.ActivationFunctionType
ALU = mybir.AluOpType

P = 128
T = 1024
C = 16
D = 2048
HID = 5632
HC = 44
SEGH = 22
NSEG = 2
NB = 512
EPSV = 1e-5

OFF_SM = 0
OFF_SM2 = 736
OFF_QS = 800
OFF_MBOX = 816
OFF_MG = 944
OFF_CARRY = 1456
OFF_CT1 = 1520
OFF_CT2 = 1584
OFF_ONESR = 1648
OFF_ONESRB = 2160
OFF_ONESC = 2416
OFF_ONESCB = 2420
OFF_EPS = 2424
OFF_CROW = 2432
OFF_CROWD = 4480
OFF_H = 6656
OFF_BIG = 39424
OFF_WS = 137728
OFF_A1 = 162304
OFF_A2 = 166400
OFF_A3 = 170496
OFF_A4 = 174592
OFF_A5 = 178688
OFF_A6 = 182784
OFF_A7 = 186880
TOTAL_BYTES = 207360

SM_BIN = 0
SM_BDEC = 16
SM_DBIAS = 32
SM_BGATE = 48
SM_BOUT = 64
SM_SDOWN = 80
SM_SGATE = 96
SM_SUP = 140


def build_program():
    nc = bass.Bass("TRN2", target_bir_lowering=False)

    xq = nc.dram_tensor("xq", [D, T], F32, kind="ExternalInput").ap()
    w_in = nc.dram_tensor("w_in", [D, D], F32, kind="ExternalInput").ap()
    w_dec = nc.dram_tensor("w_dec", [D, D], F32, kind="ExternalInput").ap()
    w_gate = nc.dram_tensor("w_gate", [D, D], F32, kind="ExternalInput").ap()
    w_out = nc.dram_tensor("w_out", [D, D], F32, kind="ExternalInput").ap()
    wg_t = nc.dram_tensor("wg_t", [D, HID], I32, kind="ExternalInput").ap()
    wu_t = nc.dram_tensor("wu_t", [D, HID], I32, kind="ExternalInput").ap()
    wd_t = nc.dram_tensor("wd_t", [HID, D], I32, kind="ExternalInput").ap()
    smalls = nc.dram_tensor("smalls", [P, 184], F32, kind="ExternalInput").ap()
    qsel = nc.dram_tensor("qsel", [P, 4], F32, kind="ExternalInput").ap()
    outT = nc.dram_tensor("outT", [D, T], F32, kind="ExternalOutput").ap()
    cc_in = nc.dram_tensor("cc_in", [P, 32], F32)
    cc_out = nc.dram_tensor("cc_out", [4 * P, 32], F32)

    xq3 = xq.rearrange("(c p) t -> p c t", p=P)
    out3 = outT.rearrange("(c p) t -> p c t", p=P)
    w_in3 = w_in.rearrange("(kc p) e -> p kc e", p=P)
    w_dec3 = w_dec.rearrange("(kc p) e -> p kc e", p=P)
    w_gate3 = w_gate.rearrange("(kc p) e -> p kc e", p=P)
    w_out3 = w_out.rearrange("(kc p) e -> p kc e", p=P)
    wg3 = wg_t.rearrange("(kc p) h -> p kc h", p=P)
    wu3 = wu_t.rearrange("(kc p) h -> p kc h", p=P)
    wd3 = wd_t.rearrange("(hc p) d -> p hc d", p=P)
    ccg = cc_out.rearrange("(r p) j -> p r j", p=P)

    with ExitStack() as ctx:
        AR = ctx.enter_context(nc.sbuf_tensor("AR", [P, TOTAL_BYTES // 2], BF16))

        def cv_f32(off_b, rows, cols):
            v = AR[:, off_b // 2:(off_b + cols * 4) // 2].bitcast(F32)
            return v[0:rows, :] if rows != P else v

        def cv_bf(off_b, rows, cols):
            v = AR[:, off_b // 2:(off_b + cols * 2) // 2]
            return v[0:rows, :] if rows != P else v

        SM = cv_f32(OFF_SM, P, 184)
        SM2 = cv_f32(OFF_SM2, P, 16)
        QS = cv_f32(OFF_QS, P, 4)
        MBOX = cv_f32(OFF_MBOX, P, 32)
        MG = cv_f32(OFF_MG, P, 128)
        CARRY = cv_f32(OFF_CARRY, P, 16)
        CT1 = cv_f32(OFF_CT1, P, 16)
        CT2 = cv_f32(OFF_CT2, P, 16)
        ONESR = cv_f32(OFF_ONESR, 1, 128)
        ONESRB = cv_bf(OFF_ONESRB, 1, 128)
        ONESC = cv_f32(OFF_ONESC, P, 1)
        ONESCB = cv_bf(OFF_ONESCB, P, 1)
        EPS = cv_f32(OFF_EPS, 1, 1)
        CROW = cv_bf(OFF_CROW, 1, 1024)
        CROWD = cv_bf(OFF_CROWD, 1, 1024)
        H = cv_bf(OFF_H, P, C * T)
        XN = cv_f32(OFF_BIG, P, C * T)

        def z1t(c):
            return cv_bf(OFF_BIG + c * 2048, P, T)

        def z2t(c):
            return cv_bf(OFF_BIG + 32768 + c * 2048, P, T)

        def yt(c):
            return cv_bf(OFF_BIG + 65536 + c * 2048, P, T)

        def gut(j):
            if j < 16:
                return yt(j)
            return cv_bf(OFF_A7 + (j - 16) * 2048, P, T)

        WS = [cv_bf(OFF_WS + j * 4096, P, 2048) for j in range(6)]
        WD = [cv_bf(OFF_A3, P, SEGH * 128), cv_bf(OFF_A5, P, SEGH * 128)]
        XS = [cv_f32(OFF_A1, P, T), cv_f32(OFF_A2, P, T)]
        OUTB = XS
        SQ = [cv_f32(OFF_A3, P, T), cv_f32(OFF_A4, P, T)]
        SCRD = cv_f32(OFF_A3, P, T)
        SCRB = cv_f32(OFF_A4, P, T)
        SCRT = cv_f32(OFF_A5, P, T)
        ZEROS = cv_f32(OFF_A6, P, T)
        STATES = cv_f32(OFF_A3, P, T)
        GBUF = cv_f32(OFF_A5, P, T)
        T1 = cv_f32(OFF_A7, 1, 1024)
        T2 = cv_f32(OFF_A7 + 4096, 1, 1024)
        ROW3 = cv_f32(OFF_A7 + 8192, 1, 1024)
        AROW = cv_f32(OFF_A7 + 12288, 1, 1024)
        BROW = cv_f32(OFF_A7 + 16384, 1, 1024)
        GTMP = cv_f32(OFF_A7 + 12288, P, T)

        PS_A = ctx.enter_context(nc.psum_tensor("PS_A", [P, 1024], F32))
        PS_B = ctx.enter_context(nc.psum_tensor("PS_B", [P, 1024], F32))
        PS_C = ctx.enter_context(nc.psum_tensor("PS_C", [P, 1024], F32))
        PS_D = ctx.enter_context(nc.psum_tensor("PS_D", [P, 1024], F32))

        sem_names = (["ld", "mb", "cc", "mg", "pe", "dv", "ac",
                      "xs0", "xs1", "wd0", "wd1", "so0", "so1"]
                     + [f"ws{j}" for j in range(6)])
        SEMS = {n: ctx.enter_context(nc.semaphore(n)) for n in sem_names}
        CNT = {n: 0 for n in sem_names}

        SCHED = {"sync": [], "gpsimd": [], "vector": [], "scalar": [],
                 "tensor": []}

        def emit(eng, fn):
            SCHED[eng].append(fn)

        def wait(eng, sem, val):
            if val <= 0:
                return
            emit(eng, lambda e, h=SEMS[sem], v=val: e.wait_ge(h, v))

        def op(eng, meth, *args, inc=None, incby=1, **kw):
            if inc:
                CNT[inc] += incby
            h = SEMS[inc] if inc else None
            def fn(e, meth=meth, args=args, kw=kw, h=h, incby=incby):
                r = getattr(e, meth)(*args, **kw)
                if h is not None:
                    r.then_inc(h, incby)
            emit(eng, fn)
            return CNT[inc] if inc else None

        def dma(queue, out, in_, sem):
            return op(queue, "dma_start", inc=sem, incby=16, out=out, in_=in_)

        ws_free = [("pe", 0)] * 6
        ws_ready = [0] * 6

        def load_stripe(j, src_ap):
            s, v = ws_free[j]
            wait("gpsimd", s, v)
            ws_ready[j] = dma("gpsimd", WS[j].rearrange(
                "p (kc e) -> p kc e", e=128), src_ap, f"ws{j}")

        dma("sync", SM, smalls, "ld")
        dma("sync", QS, qsel, "ld")
        op("vector", "memset", ONESR, 1.0)
        op("vector", "memset", ONESRB, 1.0)
        op("vector", "memset", ONESC, 1.0)
        op("vector", "memset", ONESCB, 1.0)
        op("vector", "memset", EPS, EPSV)
        op("vector", "memset", ZEROS, 0.0)
        wait("vector", "ld", 32)
        op("vector", "tensor_tensor", SM2, SM[:, 16:32], SM[:, 32:48], ALU.add)

        ph0_done = [0] * C
        sq_done = [0] * C
        xs_ready = [0, 0]
        for c in range(C):
            j = c % 2
            if c >= 2:
                wait("sync", "pe", ph0_done[c - 2])
                wait("sync", "ac", sq_done[c - 2])
            xs_ready[j] = dma("sync", XS[j], xq3[:, c, :], f"xs{j}")
            wait("scalar", f"xs{j}", xs_ready[j])
            if c >= 2:
                wait("scalar", "pe", ph0_done[c - 2])
            sq_done[c] = op("scalar", "activation", SQ[j], XS[j], AF.Square,
                            bias=0.0, scale=1.0, inc="ac")
            wait("tensor", f"xs{j}", xs_ready[j])
            for n in range(2):
                op("tensor", "matmul", PS_A[0:1, n * NB:(n + 1) * NB], ONESC,
                   XS[j][:, n * NB:(n + 1) * NB], start=(c == 0),
                   stop=(c == C - 1))
            wait("tensor", "ac", sq_done[c])
            for n in range(2):
                last = (n == 1)
                ph0_done[c] = op("tensor", "matmul",
                                 PS_C[0:1, n * NB:(n + 1) * NB], ONESC,
                                 SQ[j][:, n * NB:(n + 1) * NB],
                                 start=(c == 0), stop=(c == C - 1),
                                 inc="pe" if last else None)

        def stats_rows(src_s1, src_s2):
            op("vector", "tensor_scalar", T1, src_s1, 1.0 / D, None, ALU.mult)
            op("vector", "tensor_scalar", T2, src_s2, 1.0 / D, None, ALU.mult)
            op("vector", "tensor_tensor", ROW3, T1, T1, ALU.mult)
            dvn = op("vector", "tensor_tensor", T2, T2, ROW3, ALU.subtract,
                     inc="dv")
            wait("scalar", "dv", dvn)
            acn = op("scalar", "activation", ROW3, T2, AF.Sqrt, bias=EPS,
                     scale=1.0, inc="ac")
            wait("vector", "ac", acn)
            op("vector", "reciprocal", AROW, ROW3)
            return op("vector", "scalar_tensor_tensor", BROW, T1, -1.0, AROW,
                      ALU.mult, ALU.mult, inc="dv")

        wait("vector", "pe", ph0_done[C - 1])
        rows1 = stats_rows(PS_A[0:1, :], PS_C[0:1, :])

        wait("tensor", "dv", rows1)
        for n in range(2):
            op("tensor", "matmul", PS_B[:, n * NB:(n + 1) * NB], ONESR,
               AROW[0:1, n * NB:(n + 1) * NB], start=True, stop=True)
        bc1 = 0
        for n in range(2):
            bc1 = op("tensor", "matmul", PS_D[:, n * NB:(n + 1) * NB], ONESR,
                     BROW[0:1, n * NB:(n + 1) * NB], start=True, stop=True,
                     inc="pe" if n == 1 else None)

        h1_done = [0] * C
        for c in range(C):
            j = c % 2
            if c >= 2:
                wait("sync", "dv", h1_done[c - 2])
            else:
                wait("sync", "pe", ph0_done[C - 2 + c])
                wait("sync", "ac", sq_done[C - 2 + c])
            xs_ready[j] = dma("sync", XS[j], xq3[:, c, :], f"xs{j}")
            if c == 0:
                wait("vector", "pe", bc1)
            wait("vector", f"xs{j}", xs_ready[j])
            op("vector", "tensor_tensor", SCRT, XS[j], PS_B[:, :], ALU.mult)
            h1_done[c] = op("vector", "tensor_tensor",
                            H[:, c * T:(c + 1) * T], SCRT, PS_D[:, :],
                            ALU.add, inc="dv")

        pe_inp = [0] * C
        pe_tile = [0] * C
        act_tile = [0] * C
        dv_tile = [0] * C
        for c in range(C):
            ji, jd = (2 * c) % 6, (2 * c + 1) % 6
            load_stripe(ji, w_in3[:, :, c * 128:(c + 1) * 128])
            load_stripe(jd, w_dec3[:, :, c * 128:(c + 1) * 128])
            p0, p1 = (PS_A, PS_B) if c % 2 == 0 else (PS_C, PS_D)
            wait("tensor", f"ws{ji}", ws_ready[ji])
            wait("tensor", f"ws{jd}", ws_ready[jd])
            if c == 0:
                wait("tensor", "dv", h1_done[C - 1])
            if c >= 2:
                wait("tensor", "ac", act_tile[c - 2])
            for n in range(2):
                for k in range(C):
                    pe_inp[c] = op(
                        "tensor", "matmul", p0[:, n * NB:(n + 1) * NB],
                        WS[ji][:, k * 128:(k + 1) * 128],
                        H[:, k * T + n * NB:k * T + n * NB + NB],
                        start=(k == 0), stop=(k == C - 1),
                        inc="pe" if (n == 1 and k == C - 1) else None)
            for n in range(2):
                for k in range(C):
                    pe_tile[c] = op(
                        "tensor", "matmul", p1[:, n * NB:(n + 1) * NB],
                        WS[jd][:, k * 128:(k + 1) * 128],
                        H[:, k * T + n * NB:k * T + n * NB + NB],
                        start=(k == 0), stop=(k == C - 1),
                        inc="pe" if (n == 1 and k == C - 1) else None)
            ws_free[ji] = ("pe", pe_tile[c])
            ws_free[jd] = ("pe", pe_tile[c])

            if c >= 1:
                wait("scalar", "dv", dv_tile[c - 1])
            wait("scalar", "pe", pe_inp[c])
            op("scalar", "activation", SCRB, p0[:, :], AF.Silu,
               bias=SM[:, SM_BIN + c:SM_BIN + c + 1], scale=1.0)
            wait("scalar", "pe", pe_tile[c])
            act_tile[c] = op("scalar", "activation", SCRD, p1[:, :],
                             AF.Sigmoid, bias=SM2[:, c:c + 1], scale=1.0,
                             inc="ac")

            wait("vector", "ac", act_tile[c])
            op("vector", "tensor_tensor", SCRT, SCRD, SCRB, ALU.mult)
            op("vector", "tensor_tensor", SCRB, SCRB, SCRT, ALU.subtract)
            op("vector", "tensor_tensor_scan", SCRT, SCRD, SCRB, 0.0,
               ALU.mult, ALU.add)
            op("vector", "tensor_tensor_scan", SCRB, SCRD, ZEROS, 1.0,
               ALU.mult, ALU.add)
            op("vector", "tensor_copy", MBOX[:, c:c + 1], SCRT[:, T - 1:T])
            op("vector", "tensor_copy", MBOX[:, 16 + c:17 + c],
               SCRB[:, T - 1:T])
            op("vector", "tensor_copy", z1t(c), SCRT)
            dv_tile[c] = op("vector", "tensor_copy", z2t(c), SCRB, inc="dv")

        wait("sync", "dv", dv_tile[C - 1])
        mb = dma("sync", cc_in.ap(), MBOX, "mb")
        wait("gpsimd", "mb", mb)
        op("gpsimd", "collective_compute", "AllGather", ALU.bypass,
           replica_groups=[[0, 1, 2, 3], [4, 5, 6, 7]],
           ins=[cc_in.ap().opt()], outs=[cc_out.ap().opt()], inc="cc")
        wait("sync", "cc", CNT["cc"])
        mg = dma("sync", MG.rearrange("p (r j) -> p r j", j=32), ccg, "mg")
        wait("vector", "mg", mg)
        op("vector", "memset", CARRY, 0.0)
        op("vector", "memset", CT1, 0.0)
        for r in range(3):
            op("vector", "tensor_tensor", CT2, MG[:, r * 32 + 16:r * 32 + 32],
               CT1, ALU.mult)
            op("vector", "tensor_tensor", CT1, MG[:, r * 32:r * 32 + 16], CT2,
               ALU.add)
            op("vector", "scalar_tensor_tensor", CARRY, CT1,
               QS[:, r + 1:r + 2], CARRY, ALU.mult, ALU.add)

        pe_gate = [0] * C
        gate_evac = [0] * C
        a2dv = [0] * C
        for c in range(C):
            jg = (32 + c) % 6
            load_stripe(jg, w_gate3[:, :, c * 128:(c + 1) * 128])
            pg = PS_A if c % 2 == 0 else PS_C
            wait("tensor", f"ws{jg}", ws_ready[jg])
            if c >= 2:
                wait("tensor", "ac", gate_evac[c - 2])
            for n in range(2):
                for k in range(C):
                    pe_gate[c] = op(
                        "tensor", "matmul", pg[:, n * NB:(n + 1) * NB],
                        WS[jg][:, k * 128:(k + 1) * 128],
                        H[:, k * T + n * NB:k * T + n * NB + NB],
                        start=(k == 0), stop=(k == C - 1),
                        inc="pe" if (n == 1 and k == C - 1) else None)
            ws_free[jg] = ("pe", pe_gate[c])

            wait("scalar", "pe", pe_gate[c])
            if c >= 1:
                wait("scalar", "dv", a2dv[c - 1])
            gate_evac[c] = op("scalar", "activation", GBUF, pg[:, :],
                              AF.Sigmoid,
                              bias=SM[:, SM_BGATE + c:SM_BGATE + c + 1],
                              scale=1.0, inc="ac")

            wait("vector", "ac", gate_evac[c])
            op("vector", "scalar_tensor_tensor", STATES, z2t(c),
               CARRY[:, c:c + 1], z1t(c), ALU.mult, ALU.add)
            a2dv[c] = op("vector", "tensor_tensor", yt(c), GBUF, STATES,
                         ALU.mult, inc="dv")

        pe_out = [0] * C
        d_evac = [0] * C
        for e in range(C):
            jo = (48 + e) % 6
            load_stripe(jo, w_out3[:, :, e * 128:(e + 1) * 128])
            j = e % 2
            if e >= 2:
                wait("sync", "dv", d_evac[e - 2])
            else:
                wait("sync", "dv", h1_done[C - 2 + e])
            xs_ready[j] = dma("sync", XS[j], xq3[:, e, :], f"xs{j}")

            po = PS_A if e % 2 == 0 else PS_C
            wait("tensor", f"ws{jo}", ws_ready[jo])
            if e == 0:
                wait("tensor", "dv", a2dv[C - 1])
            if e >= 2:
                wait("tensor", "dv", d_evac[e - 2])
            for n in range(2):
                for c in range(C):
                    pe_out[e] = op(
                        "tensor", "matmul", po[:, n * NB:(n + 1) * NB],
                        WS[jo][:, c * 128:(c + 1) * 128],
                        yt(c)[:, n * NB:(n + 1) * NB],
                        start=(c == 0), stop=(c == C - 1),
                        inc="pe" if (n == 1 and c == C - 1) else None)
            ws_free[jo] = ("pe", pe_out[e])

            wait("vector", "pe", pe_out[e])
            wait("vector", f"xs{j}", xs_ready[j])
            d_evac[e] = op("vector", "scalar_tensor_tensor",
                           XN[:, e * T:(e + 1) * T], po[:, :],
                           SM[:, SM_BOUT + e:SM_BOUT + e + 1], XS[j],
                           ALU.add, ALU.add, inc="dv")

        e1_done = [0] * C
        sq2_done = [0] * C
        for c in range(C):
            j = c % 2
            wait("scalar", "dv", d_evac[c])
            if c >= 2:
                wait("scalar", "pe", e1_done[c - 2])
            sq2_done[c] = op("scalar", "activation", SQ[j],
                             XN[:, c * T:(c + 1) * T], AF.Square, bias=0.0,
                             scale=1.0, inc="ac")
            if c == 0:
                wait("tensor", "dv", d_evac[C - 1])
            for n in range(2):
                op("tensor", "matmul", PS_A[0:1, n * NB:(n + 1) * NB], ONESC,
                   XN[:, c * T + n * NB:c * T + n * NB + NB], start=(c == 0),
                   stop=(c == C - 1))
            wait("tensor", "ac", sq2_done[c])
            for n in range(2):
                e1_done[c] = op("tensor", "matmul",
                                PS_C[0:1, n * NB:(n + 1) * NB], ONESC,
                                SQ[j][:, n * NB:(n + 1) * NB], start=(c == 0),
                                stop=(c == C - 1),
                                inc="pe" if n == 1 else None)

        wait("vector", "pe", e1_done[C - 1])
        rows2 = stats_rows(PS_A[0:1, :], PS_C[0:1, :])
        wait("tensor", "dv", rows2)
        for n in range(2):
            op("tensor", "matmul", PS_B[:, n * NB:(n + 1) * NB], ONESR,
               AROW[0:1, n * NB:(n + 1) * NB], start=True, stop=True)
        bc2 = 0
        for n in range(2):
            bc2 = op("tensor", "matmul", PS_D[:, n * NB:(n + 1) * NB], ONESR,
                     BROW[0:1, n * NB:(n + 1) * NB], start=True, stop=True,
                     inc="pe" if n == 1 else None)

        h2_done = 0
        for c in range(C):
            if c == 0:
                wait("vector", "pe", bc2)
            op("vector", "tensor_tensor", SCRT, XN[:, c * T:(c + 1) * T],
               PS_B[:, :], ALU.mult)
            h2_done = op("vector", "tensor_tensor", H[:, c * T:(c + 1) * T],
                         SCRT, PS_D[:, :], ALU.add, inc="dv")

        wait("tensor", "dv", h2_done)
        csh = 0
        for c in range(C):
            for n in range(2):
                csh = op("tensor", "matmul", PS_A[0:1, n * NB:(n + 1) * NB],
                         ONESCB, H[:, c * T + n * NB:c * T + n * NB + NB],
                         start=(c == 0), stop=(c == C - 1),
                         inc="pe" if (c == C - 1 and n == 1) else None)
        wait("vector", "pe", csh)
        crow_done = op("vector", "tensor_scalar", CROW, PS_A[0:1, :], -128.0,
                       None, ALU.mult, inc="dv")

        wd_free = [("pe", e1_done[C - 1]), ("dv", h2_done)]
        wd_ready = [0, 0]
        so_cnt = [0, 0]
        outb_free = [0, 0]
        for seg in range(NSEG):
            g_done = [0] * SEGH
            u_done = [0] * SEGH
            gevac = [0] * SEGH
            guevac = [0] * SEGH
            for jj in range(SEGH):
                h = seg * SEGH + jj
                jg, ju = (2 * h) % 6, (2 * h + 1) % 6
                load_stripe(jg, wg3[:, :, h * 128:(h + 1) * 128])
                load_stripe(ju, wu3[:, :, h * 128:(h + 1) * 128])
                pg = PS_A if jj % 2 == 0 else PS_C
                pu = PS_B if jj % 2 == 0 else PS_D
                wait("tensor", f"ws{jg}", ws_ready[jg])
                wait("tensor", f"ws{ju}", ws_ready[ju])
                if seg == 0 and jj == 0:
                    wait("tensor", "dv", crow_done)
                if seg == 1 and jj == 0:
                    wait("tensor", "dv", prev_devac15)
                if jj >= 2:
                    wait("tensor", "ac", gevac[jj - 2])
                    wait("tensor", "dv", guevac[jj - 2])
                for n in range(2):
                    for k in range(C):
                        op("tensor", "matmul", pg[:, n * NB:(n + 1) * NB],
                           WS[jg][:, k * 128:(k + 1) * 128],
                           H[:, k * T + n * NB:k * T + n * NB + NB],
                           start=(k == 0), stop=False)
                    g_done[jj] = op("tensor", "matmul",
                                    pg[:, n * NB:(n + 1) * NB], ONESRB,
                                    CROW[0:1, n * NB:(n + 1) * NB],
                                    start=False, stop=True,
                                    inc="pe" if n == 1 else None)
                for n in range(2):
                    for k in range(C):
                        op("tensor", "matmul", pu[:, n * NB:(n + 1) * NB],
                           WS[ju][:, k * 128:(k + 1) * 128],
                           H[:, k * T + n * NB:k * T + n * NB + NB],
                           start=(k == 0), stop=False)
                    u_done[jj] = op("tensor", "matmul",
                                    pu[:, n * NB:(n + 1) * NB], ONESRB,
                                    CROW[0:1, n * NB:(n + 1) * NB],
                                    start=False, stop=True,
                                    inc="pe" if n == 1 else None)
                ws_free[jg] = ("pe", u_done[jj])
                ws_free[ju] = ("pe", u_done[jj])

                wait("scalar", "pe", g_done[jj])
                if jj >= 1:
                    wait("scalar", "dv", guevac[jj - 1])
                elif seg == 1:
                    wait("scalar", "dv", prev_guevac[SEGH - 1])
                gevac[jj] = op("scalar", "activation", GTMP, pg[:, :],
                               AF.Silu, bias=0.0,
                               scale=SM[:, SM_SGATE + h:SM_SGATE + h + 1],
                               inc="ac")

                wait("vector", "pe", u_done[jj])
                wait("vector", "ac", gevac[jj])
                guevac[jj] = op("vector", "scalar_tensor_tensor", gut(jj),
                                pu[:, :], SM[:, SM_SUP + h:SM_SUP + h + 1],
                                GTMP, ALU.mult, ALU.mult, inc="dv")

            wait("tensor", "dv", guevac[SEGH - 1])
            csg = 0
            for jj in range(SEGH):
                for n in range(2):
                    csg = op("tensor", "matmul",
                             PS_B[0:1, n * NB:(n + 1) * NB], ONESCB,
                             gut(jj)[:, n * NB:(n + 1) * NB],
                             start=(jj == 0), stop=(jj == SEGH - 1),
                             inc="pe" if (jj == SEGH - 1 and n == 1) else None)
            wait("vector", "pe", csg)
            crowd_done = op("vector", "tensor_scalar", CROWD, PS_B[0:1, :],
                            -128.0, None, ALU.mult, inc="dv")

            down_done = [0] * C
            devac = [0] * C
            for d in range(C):
                j = d % 2
                s, v = wd_free[j]
                wait("gpsimd", s, v)
                wd_ready[j] = dma(
                    "gpsimd", WD[j].rearrange("p (hc dd) -> p hc dd", dd=128),
                    wd3[:, seg * SEGH:(seg + 1) * SEGH,
                        d * 128:(d + 1) * 128], f"wd{j}")

                pd = PS_A if d % 2 == 0 else PS_C
                wait("tensor", f"wd{j}", wd_ready[j])
                if d == 0:
                    wait("tensor", "dv", crowd_done)
                if d >= 2:
                    wait("tensor", "dv", devac[d - 2])
                for n in range(2):
                    for jj in range(SEGH):
                        op("tensor", "matmul", pd[:, n * NB:(n + 1) * NB],
                           WD[j][:, jj * 128:(jj + 1) * 128],
                           gut(jj)[:, n * NB:(n + 1) * NB],
                           start=(jj == 0), stop=False)
                    down_done[d] = op("tensor", "matmul",
                                      pd[:, n * NB:(n + 1) * NB], ONESRB,
                                      CROWD[0:1, n * NB:(n + 1) * NB],
                                      start=False, stop=True,
                                      inc="pe" if n == 1 else None)
                wd_free[j] = ("pe", down_done[d])

                wait("vector", "pe", down_done[d])
                if seg == 0:
                    devac[d] = op("vector", "scalar_tensor_tensor",
                                  XN[:, d * T:(d + 1) * T], pd[:, :],
                                  SM[:, SM_SDOWN + d:SM_SDOWN + d + 1],
                                  XN[:, d * T:(d + 1) * T], ALU.mult, ALU.add,
                                  inc="dv")
                else:
                    wait("vector", f"so{j}", outb_free[j])
                    devac[d] = op("vector", "scalar_tensor_tensor", OUTB[j],
                                  pd[:, :],
                                  SM[:, SM_SDOWN + d:SM_SDOWN + d + 1],
                                  XN[:, d * T:(d + 1) * T], ALU.mult, ALU.add,
                                  inc="dv")
                    wait("sync", "dv", devac[d])
                    so_cnt[j] = dma("sync", out3[:, d, :], OUTB[j], f"so{j}")
                    outb_free[j] = so_cnt[j]
            prev_gevac = gevac
            prev_guevac = guevac
            prev_devac15 = devac[C - 1]

        wait("sync", "so0", so_cnt[0])
        wait("sync", "so1", so_cnt[1])

        with nc.Block() as block:
            @block.sync
            def _(e):
                for fn in SCHED["sync"]:
                    fn(e)

            @block.gpsimd
            def _(e):
                for fn in SCHED["gpsimd"]:
                    fn(e)

            @block.vector
            def _(e):
                for fn in SCHED["vector"]:
                    fn(e)

            @block.scalar
            def _(e):
                for fn in SCHED["scalar"]:
                    fn(e)

            @block.tensor
            def _(e):
                for fn in SCHED["tensor"]:
                    fn(e)

    return nc


def kernel(**inputs):
    x = np.asarray(inputs["L0_attn_x"], dtype=np.float32)
    w_in = np.ascontiguousarray(np.asarray(inputs["L0_attn_W_in"], dtype=np.float32))
    w_dec = np.ascontiguousarray(np.asarray(inputs["L0_attn_W_decay"], dtype=np.float32))
    w_gate = np.ascontiguousarray(np.asarray(inputs["L0_attn_W_gate"], dtype=np.float32))
    w_out = np.ascontiguousarray(np.asarray(inputs["L0_attn_W_out"], dtype=np.float32))
    wg_t = np.ascontiguousarray(np.asarray(inputs["L0_ffn_wq_gate"], dtype=np.int32).T)
    wu_t = np.ascontiguousarray(np.asarray(inputs["L0_ffn_wq_up"], dtype=np.int32).T)
    wd_t = np.ascontiguousarray(np.asarray(inputs["L0_ffn_wq_down"], dtype=np.int32).T)

    def rs(v):
        return np.asarray(v, dtype=np.float32).reshape(-1, 128).T

    smalls = np.ascontiguousarray(np.concatenate(
        [rs(inputs["L0_attn_b_in"]), rs(inputs["L0_attn_b_decay"]),
         rs(inputs["L0_attn_decay_bias"]), rs(inputs["L0_attn_b_gate"]),
         rs(inputs["L0_attn_b_out"]), rs(inputs["L0_ffn_s_down"]),
         rs(inputs["L0_ffn_s_gate"]), rs(inputs["L0_ffn_s_up"])], axis=1))

    xT = np.ascontiguousarray(np.transpose(x, (0, 2, 1)))

    nc = build_program()
    in_maps = []
    for core in range(8):
        b, q = core // 4, core % 4
        qs = np.zeros((P, 4), dtype=np.float32)
        qs[:, q] = 1.0
        in_maps.append(dict(
            xq=np.ascontiguousarray(xT[b][:, q * T:(q + 1) * T]),
            w_in=w_in, w_dec=w_dec, w_gate=w_gate, w_out=w_out,
            wg_t=wg_t, wu_t=wu_t, wd_t=wd_t, smalls=smalls, qsel=qs))

    res = run_bass_kernel_spmd(nc, in_maps, core_ids=list(range(8)))

    outT = np.empty((2, D, 4096), dtype=np.float32)
    for core in range(8):
        b, q = core // 4, core % 4
        outT[b][:, q * T:(q + 1) * T] = np.asarray(res.results[core]["outT"])
    return np.ascontiguousarray(np.transpose(outT, (0, 2, 1)))


# version 02
# speedup vs baseline: 1.0917x; 1.0917x over previous
import numpy as np
import concourse.bass as bass
import concourse.mybir as mybir
from concourse.bass_utils import run_bass_kernel_spmd
from contextlib import ExitStack

F32 = mybir.dt.float32
BF16 = mybir.dt.bfloat16
I32 = mybir.dt.int32
AF = mybir.ActivationFunctionType
ALU = mybir.AluOpType

P = 128
T = 1024
C = 16
D = 2048
HID = 5632
HC = 44
SEGH = 22
NSEG = 2
NB = 512
EPSV = 1e-5

OFF_SM = 0
OFF_SM2 = 736
OFF_QS = 800
OFF_MBOX = 816
OFF_MG = 944
OFF_CARRY = 1456
OFF_CT1 = 1520
OFF_CT2 = 1584
OFF_ONESR = 1648
OFF_ONESRB = 2160
OFF_ONESC = 2416
OFF_ONESCB = 2420
OFF_EPS = 2424
OFF_CROWD = 4480
OFF_H = 6656
OFF_BIG = 39424
OFF_WS = 137728
OFF_A1 = 162304
OFF_A2 = 166400
OFF_A3 = 170496
OFF_A4 = 174592
OFF_A5 = 178688
OFF_A6 = 182784
OFF_A7 = 186880
TOTAL_BYTES = 207360

SM_BIN = 0
SM_BDEC = 16
SM_DBIAS = 32
SM_BGATE = 48
SM_BOUT = 64
SM_SDOWN = 80
SM_SGATE = 96
SM_SUP = 140


def build_program():
    nc = bass.Bass("TRN2", target_bir_lowering=False)

    xq = nc.dram_tensor("xq", [D, T], F32, kind="ExternalInput").ap()
    w_in = nc.dram_tensor("w_in", [C, P, 2048], F32, kind="ExternalInput").ap()
    w_dec = nc.dram_tensor("w_dec", [C, P, 2048], F32, kind="ExternalInput").ap()
    w_gate = nc.dram_tensor("w_gate", [C, P, 2048], F32, kind="ExternalInput").ap()
    w_out = nc.dram_tensor("w_out", [C, P, 2048], F32, kind="ExternalInput").ap()
    wg_t = nc.dram_tensor("wg_t", [HC, P, 2048], I32, kind="ExternalInput").ap()
    wu_t = nc.dram_tensor("wu_t", [HC, P, 2048], I32, kind="ExternalInput").ap()
    wd_t = nc.dram_tensor("wd_t", [NSEG, C, P, SEGH * 128], I32,
                          kind="ExternalInput").ap()
    smalls = nc.dram_tensor("smalls", [P, 184], F32, kind="ExternalInput").ap()
    qsel = nc.dram_tensor("qsel", [P, 4], F32, kind="ExternalInput").ap()
    outT = nc.dram_tensor("outT", [D, T], F32, kind="ExternalOutput").ap()
    cc_in = nc.dram_tensor("cc_in", [P, 32], F32)
    cc_out = nc.dram_tensor("cc_out", [4 * P, 32], F32)

    xq3 = xq.rearrange("(c p) t -> p c t", p=P)
    out3 = outT.rearrange("(c p) t -> p c t", p=P)
    ccg = cc_out.rearrange("(r p) j -> p r j", p=P)

    with ExitStack() as ctx:
        AR = ctx.enter_context(nc.sbuf_tensor("AR", [P, TOTAL_BYTES // 2], BF16))

        def cv_f32(off_b, rows, cols):
            v = AR[:, off_b // 2:(off_b + cols * 4) // 2].bitcast(F32)
            return v[0:rows, :] if rows != P else v

        def cv_bf(off_b, rows, cols):
            v = AR[:, off_b // 2:(off_b + cols * 2) // 2]
            return v[0:rows, :] if rows != P else v

        SM = cv_f32(OFF_SM, P, 184)
        SM2 = cv_f32(OFF_SM2, P, 16)
        QS = cv_f32(OFF_QS, P, 4)
        MBOX = cv_f32(OFF_MBOX, P, 32)
        MG = cv_f32(OFF_MG, P, 128)
        CARRY = cv_f32(OFF_CARRY, P, 16)
        CT1 = cv_f32(OFF_CT1, P, 16)
        CT2 = cv_f32(OFF_CT2, P, 16)
        ONESR = cv_f32(OFF_ONESR, 1, 128)
        ONESRB = cv_bf(OFF_ONESRB, 1, 128)
        ONESCB = cv_bf(OFF_ONESCB, P, 1)
        EPS = cv_f32(OFF_EPS, 1, 1)
        CROWD = cv_bf(OFF_CROWD, 1, 1024)
        H = cv_bf(OFF_H, P, C * T)
        XN = cv_f32(OFF_BIG, P, C * T)

        def z1t(c):
            return cv_bf(OFF_BIG + c * 2048, P, T)

        def z2t(c):
            return cv_bf(OFF_BIG + 32768 + c * 2048, P, T)

        def yt(c):
            return cv_bf(OFF_BIG + 65536 + c * 2048, P, T)

        def gut(j):
            if j < 16:
                return yt(j)
            return cv_bf(OFF_A7 + (j - 16) * 2048, P, T)

        WS = [cv_bf(OFF_WS + j * 4096, P, 2048) for j in range(6)]
        WD = [cv_bf(OFF_A3, P, SEGH * 128), cv_bf(OFF_A5, P, SEGH * 128)]
        XS = [cv_f32(OFF_A1, P, T), cv_f32(OFF_A2, P, T)]
        OUTB = XS
        SQB = [cv_bf(OFF_A3, P, T), cv_bf(OFF_A3 + 2048, P, T)]
        ABC = cv_f32(OFF_A3, P, T)
        BBC = cv_f32(OFF_A4, P, T)
        SCRD = cv_f32(OFF_A3, P, T)
        SCRB = cv_f32(OFF_A4, P, T)
        SCRT = cv_f32(OFF_A5, P, T)
        ZEROS = cv_f32(OFF_A6, P, T)
        STATES = cv_f32(OFF_A3, P, T)
        GBUF = cv_f32(OFF_A5, P, T)
        T1 = cv_f32(OFF_A7, 1, 1024)
        T2 = cv_f32(OFF_A7 + 4096, 1, 1024)
        ROW3 = cv_f32(OFF_A7 + 8192, 1, 1024)
        AROW = cv_f32(OFF_A7 + 12288, 1, 1024)
        BROW = cv_f32(OFF_A7 + 16384, 1, 1024)
        GTMP = cv_f32(OFF_A7 + 12288, P, T)

        PS_A = ctx.enter_context(nc.psum_tensor("PS_A", [P, 1024], F32))
        PS_B = ctx.enter_context(nc.psum_tensor("PS_B", [P, 1024], F32))
        PS_C = ctx.enter_context(nc.psum_tensor("PS_C", [P, 1024], F32))
        PS_D = ctx.enter_context(nc.psum_tensor("PS_D", [P, 1024], F32))

        sem_names = (["ld", "mb", "cc", "mg", "pe", "dv", "ac",
                      "xs0", "xs1", "wd0", "wd1", "so0", "so1"]
                     + [f"ws{j}" for j in range(6)])
        SEMS = {n: ctx.enter_context(nc.semaphore(n)) for n in sem_names}
        CNT = {n: 0 for n in sem_names}

        SCHED = {"sync": [], "gpsimd": [], "vector": [], "scalar": [],
                 "tensor": []}

        def emit(eng, fn):
            SCHED[eng].append(fn)

        def wait(eng, sem, val):
            if val <= 0:
                return
            emit(eng, lambda e, h=SEMS[sem], v=val: e.wait_ge(h, v))

        def op(eng, meth, *args, inc=None, incby=1, **kw):
            if inc:
                CNT[inc] += incby
            h = SEMS[inc] if inc else None
            def fn(e, meth=meth, args=args, kw=kw, h=h, incby=incby):
                r = getattr(e, meth)(*args, **kw)
                if h is not None:
                    r.then_inc(h, incby)
            emit(eng, fn)
            return CNT[inc] if inc else None

        def dma(queue, out, in_, sem):
            return op(queue, "dma_start", inc=sem, incby=16, out=out, in_=in_)

        ws_free = [("pe", 0)] * 6
        ws_ready = [0] * 6

        def load_stripe(j, src_ap):
            s, v = ws_free[j]
            wait("gpsimd", s, v)
            ws_ready[j] = dma("gpsimd", WS[j][:, :], src_ap, f"ws{j}")

        dma("sync", SM, smalls, "ld")
        dma("sync", QS, qsel, "ld")
        op("vector", "memset", ONESR, 1.0)
        op("vector", "memset", ONESRB, 1.0)
        op("vector", "memset", ONESCB, 1.0)
        op("vector", "memset", EPS, EPSV)
        op("vector", "memset", ZEROS, 0.0)
        wait("vector", "ld", 32)
        op("vector", "tensor_tensor", SM2, SM[:, 16:32], SM[:, 32:48], ALU.add)

        ph0_done = [0] * C
        cast_done = [0] * C
        sq_done = [0] * C
        xs_ready = [0, 0]
        for c in range(C):
            j = c % 2
            if c >= 2:
                wait("sync", "dv", cast_done[c - 2])
            xs_ready[j] = dma("sync", XS[j], xq3[:, c, :], f"xs{j}")
            wait("vector", f"xs{j}", xs_ready[j])
            cast_done[c] = op("vector", "tensor_copy", H[:, c * T:(c + 1) * T],
                              XS[j], inc="dv")
            wait("scalar", "dv", cast_done[c])
            if c >= 2:
                wait("scalar", "pe", ph0_done[c - 2])
            sq_done[c] = op("scalar", "activation", SQB[j],
                            H[:, c * T:(c + 1) * T], AF.Square,
                            bias=0.0, scale=1.0, inc="ac")
            wait("tensor", "dv", cast_done[c])
            for n in range(2):
                op("tensor", "matmul", PS_A[0:1, n * NB:(n + 1) * NB], ONESCB,
                   H[:, c * T + n * NB:c * T + n * NB + NB], start=(c == 0),
                   stop=(c == C - 1))
            wait("tensor", "ac", sq_done[c])
            for n in range(2):
                ph0_done[c] = op("tensor", "matmul",
                                 PS_C[0:1, n * NB:(n + 1) * NB], ONESCB,
                                 SQB[j][:, n * NB:(n + 1) * NB],
                                 start=(c == 0), stop=(c == C - 1),
                                 inc="pe" if n == 1 else None)

        def stats_rows(src_s1, src_s2):
            op("vector", "tensor_scalar", T1, src_s1, 1.0 / D, None, ALU.mult)
            op("vector", "tensor_scalar", T2, src_s2, 1.0 / D, None, ALU.mult)
            op("vector", "tensor_tensor", ROW3, T1, T1, ALU.mult)
            dvn = op("vector", "tensor_tensor", T2, T2, ROW3, ALU.subtract,
                     inc="dv")
            wait("scalar", "dv", dvn)
            acn = op("scalar", "activation", ROW3, T2, AF.Sqrt, bias=EPS,
                     scale=1.0, inc="ac")
            wait("vector", "ac", acn)
            op("vector", "reciprocal", AROW, ROW3)
            return op("vector", "scalar_tensor_tensor", BROW, T1, -1.0, AROW,
                      ALU.mult, ALU.mult, inc="dv")

        wait("vector", "pe", ph0_done[C - 1])
        rows1 = stats_rows(PS_A[0:1, :], PS_C[0:1, :])

        wait("tensor", "dv", rows1)
        for n in range(2):
            op("tensor", "matmul", PS_B[:, n * NB:(n + 1) * NB], ONESR,
               AROW[0:1, n * NB:(n + 1) * NB], start=True, stop=True)
        bc1 = 0
        for n in range(2):
            bc1 = op("tensor", "matmul", PS_D[:, n * NB:(n + 1) * NB], ONESR,
                     BROW[0:1, n * NB:(n + 1) * NB], start=True, stop=True,
                     inc="pe" if n == 1 else None)
        wait("vector", "pe", bc1)
        op("vector", "tensor_copy", ABC, PS_B[:, :])
        op("vector", "tensor_copy", BBC, PS_D[:, :])

        h1_done = [0] * C
        for c in range(C):
            j = c % 2
            if c >= 2:
                wait("sync", "dv", h1_done[c - 2])
            else:
                wait("sync", "dv", cast_done[C - 2 + c])
            xs_ready[j] = dma("sync", XS[j], xq3[:, c, :], f"xs{j}")
            wait("vector", f"xs{j}", xs_ready[j])
            op("vector", "tensor_tensor", SCRT, XS[j], ABC, ALU.mult)
            h1_done[c] = op("vector", "tensor_tensor",
                            H[:, c * T:(c + 1) * T], SCRT, BBC,
                            ALU.add, inc="dv")

        pe_inp = [0] * C
        pe_tile = [0] * C
        act_tile = [0] * C
        dv_tile = [0] * C
        for c in range(C):
            ji, jd = (2 * c) % 6, (2 * c + 1) % 6
            load_stripe(ji, w_in[c])
            load_stripe(jd, w_dec[c])
            p0, p1 = (PS_A, PS_B) if c % 2 == 0 else (PS_C, PS_D)
            wait("tensor", f"ws{ji}", ws_ready[ji])
            wait("tensor", f"ws{jd}", ws_ready[jd])
            if c >= 2:
                wait("tensor", "ac", act_tile[c - 2])
            for n in range(2):
                for k in range(C):
                    if c == 0 and n == 0:
                        wait("tensor", "dv", h1_done[k])
                    pe_inp[c] = op(
                        "tensor", "matmul", p0[:, n * NB:(n + 1) * NB],
                        WS[ji][:, k * 128:(k + 1) * 128],
                        H[:, k * T + n * NB:k * T + n * NB + NB],
                        start=(k == 0), stop=(k == C - 1),
                        inc="pe" if (n == 1 and k == C - 1) else None)
            for n in range(2):
                for k in range(C):
                    pe_tile[c] = op(
                        "tensor", "matmul", p1[:, n * NB:(n + 1) * NB],
                        WS[jd][:, k * 128:(k + 1) * 128],
                        H[:, k * T + n * NB:k * T + n * NB + NB],
                        start=(k == 0), stop=(k == C - 1),
                        inc="pe" if (n == 1 and k == C - 1) else None)
            ws_free[ji] = ("pe", pe_tile[c])
            ws_free[jd] = ("pe", pe_tile[c])

            if c >= 1:
                wait("scalar", "dv", dv_tile[c - 1])
            wait("scalar", "pe", pe_inp[c])
            op("scalar", "activation", SCRB, p0[:, :], AF.Silu,
               bias=SM[:, SM_BIN + c:SM_BIN + c + 1], scale=1.0)
            wait("scalar", "pe", pe_tile[c])
            act_tile[c] = op("scalar", "activation", SCRD, p1[:, :],
                             AF.Sigmoid, bias=SM2[:, c:c + 1], scale=1.0,
                             inc="ac")

            wait("vector", "ac", act_tile[c])
            op("vector", "tensor_tensor", SCRT, SCRD, SCRB, ALU.mult)
            op("vector", "tensor_tensor", SCRB, SCRB, SCRT, ALU.subtract)
            op("vector", "tensor_tensor_scan", SCRT, SCRD, SCRB, 0.0,
               ALU.mult, ALU.add)
            op("vector", "tensor_tensor_scan", SCRB, SCRD, ZEROS, 1.0,
               ALU.mult, ALU.add)
            op("vector", "tensor_copy", MBOX[:, c:c + 1], SCRT[:, T - 1:T])
            op("vector", "tensor_copy", MBOX[:, 16 + c:17 + c],
               SCRB[:, T - 1:T])
            op("vector", "tensor_copy", z1t(c), SCRT)
            dv_tile[c] = op("vector", "tensor_copy", z2t(c), SCRB, inc="dv")

        wait("sync", "dv", dv_tile[C - 1])
        mb = dma("sync", cc_in.ap(), MBOX, "mb")
        wait("gpsimd", "mb", mb)
        op("gpsimd", "collective_compute", "AllGather", ALU.bypass,
           replica_groups=[[0, 1, 2, 3], [4, 5, 6, 7]],
           ins=[cc_in.ap().opt()], outs=[cc_out.ap().opt()], inc="cc")
        wait("sync", "cc", CNT["cc"])
        mg = dma("sync", MG.rearrange("p (r j) -> p r j", j=32), ccg, "mg")
        wait("vector", "mg", mg)
        op("vector", "memset", CARRY, 0.0)
        op("vector", "memset", CT1, 0.0)
        for r in range(3):
            op("vector", "tensor_tensor", CT2, MG[:, r * 32 + 16:r * 32 + 32],
               CT1, ALU.mult)
            op("vector", "tensor_tensor", CT1, MG[:, r * 32:r * 32 + 16], CT2,
               ALU.add)
            op("vector", "scalar_tensor_tensor", CARRY, CT1,
               QS[:, r + 1:r + 2], CARRY, ALU.mult, ALU.add)

        pe_gate = [0] * C
        gate_evac = [0] * C
        a2dv = [0] * C
        for c in range(C):
            jg = (32 + c) % 6
            load_stripe(jg, w_gate[c])
            pg = PS_A if c % 2 == 0 else PS_C
            wait("tensor", f"ws{jg}", ws_ready[jg])
            if c >= 2:
                wait("tensor", "ac", gate_evac[c - 2])
            for n in range(2):
                for k in range(C):
                    pe_gate[c] = op(
                        "tensor", "matmul", pg[:, n * NB:(n + 1) * NB],
                        WS[jg][:, k * 128:(k + 1) * 128],
                        H[:, k * T + n * NB:k * T + n * NB + NB],
                        start=(k == 0), stop=(k == C - 1),
                        inc="pe" if (n == 1 and k == C - 1) else None)
            ws_free[jg] = ("pe", pe_gate[c])

            wait("scalar", "pe", pe_gate[c])
            if c >= 1:
                wait("scalar", "dv", a2dv[c - 1])
            gate_evac[c] = op("scalar", "activation", GBUF, pg[:, :],
                              AF.Sigmoid,
                              bias=SM[:, SM_BGATE + c:SM_BGATE + c + 1],
                              scale=1.0, inc="ac")

            wait("vector", "ac", gate_evac[c])
            op("vector", "scalar_tensor_tensor", STATES, z2t(c),
               CARRY[:, c:c + 1], z1t(c), ALU.mult, ALU.add)
            a2dv[c] = op("vector", "tensor_tensor", yt(c), GBUF, STATES,
                         ALU.mult, inc="dv")

        pe_out = [0] * C
        d_evac = [0] * C
        for e in range(C):
            jo = (48 + e) % 6
            load_stripe(jo, w_out[e])
            j = e % 2
            if e >= 2:
                wait("sync", "dv", d_evac[e - 2])
            else:
                wait("sync", "dv", h1_done[C - 2 + e])
            xs_ready[j] = dma("sync", XS[j], xq3[:, e, :], f"xs{j}")

            po = PS_A if e % 2 == 0 else PS_C
            wait("tensor", f"ws{jo}", ws_ready[jo])
            if e >= 2:
                wait("tensor", "dv", d_evac[e - 2])
            for n in range(2):
                for c in range(C):
                    if e == 0 and n == 0:
                        wait("tensor", "dv", a2dv[c])
                    pe_out[e] = op(
                        "tensor", "matmul", po[:, n * NB:(n + 1) * NB],
                        WS[jo][:, c * 128:(c + 1) * 128],
                        yt(c)[:, n * NB:(n + 1) * NB],
                        start=(c == 0), stop=(c == C - 1),
                        inc="pe" if (n == 1 and c == C - 1) else None)
            ws_free[jo] = ("pe", pe_out[e])

            wait("vector", "pe", pe_out[e])
            wait("vector", f"xs{j}", xs_ready[j])
            d_evac[e] = op("vector", "scalar_tensor_tensor",
                           XN[:, e * T:(e + 1) * T], po[:, :],
                           SM[:, SM_BOUT + e:SM_BOUT + e + 1], XS[j],
                           ALU.add, ALU.add, inc="dv")

        e1_done = [0] * C
        cast2_done = [0] * C
        sq2_done = [0] * C
        for c in range(C):
            j = c % 2
            cast2_done[c] = op("vector", "tensor_copy",
                               H[:, c * T:(c + 1) * T],
                               XN[:, c * T:(c + 1) * T], inc="dv")
            wait("scalar", "dv", cast2_done[c])
            if c >= 2:
                wait("scalar", "pe", e1_done[c - 2])
            sq2_done[c] = op("scalar", "activation", SQB[j],
                             H[:, c * T:(c + 1) * T], AF.Square, bias=0.0,
                             scale=1.0, inc="ac")
            if c == 0:
                wait("tensor", "dv", d_evac[C - 1])
            wait("tensor", "dv", cast2_done[c])
            for n in range(2):
                op("tensor", "matmul", PS_A[0:1, n * NB:(n + 1) * NB], ONESCB,
                   H[:, c * T + n * NB:c * T + n * NB + NB], start=(c == 0),
                   stop=(c == C - 1))
            wait("tensor", "ac", sq2_done[c])
            for n in range(2):
                e1_done[c] = op("tensor", "matmul",
                                PS_C[0:1, n * NB:(n + 1) * NB], ONESCB,
                                SQB[j][:, n * NB:(n + 1) * NB], start=(c == 0),
                                stop=(c == C - 1),
                                inc="pe" if n == 1 else None)

        wait("vector", "pe", e1_done[C - 1])
        rows2 = stats_rows(PS_A[0:1, :], PS_C[0:1, :])
        wait("tensor", "dv", rows2)
        for n in range(2):
            op("tensor", "matmul", PS_B[:, n * NB:(n + 1) * NB], ONESR,
               AROW[0:1, n * NB:(n + 1) * NB], start=True, stop=True)
        bc2 = 0
        for n in range(2):
            bc2 = op("tensor", "matmul", PS_D[:, n * NB:(n + 1) * NB], ONESR,
                     BROW[0:1, n * NB:(n + 1) * NB], start=True, stop=True,
                     inc="pe" if n == 1 else None)
        wait("vector", "pe", bc2)
        op("vector", "tensor_copy", ABC, PS_B[:, :])
        abcopy2 = op("vector", "tensor_copy", BBC, PS_D[:, :], inc="dv")

        h2_done = [0] * C
        for c in range(C):
            op("vector", "tensor_tensor", SCRT, XN[:, c * T:(c + 1) * T],
               ABC, ALU.mult)
            h2_done[c] = op("vector", "tensor_tensor",
                            H[:, c * T:(c + 1) * T], SCRT, BBC, ALU.add,
                            inc="dv")

        wd_free = [("dv", h2_done[C - 1]), ("dv", h2_done[C - 1])]
        wd_ready = [0, 0]
        so_cnt = [0, 0]
        outb_free = [0, 0]
        prev_devac15 = 0
        prev_gevac = None
        prev_guevac = None
        for seg in range(NSEG):
            g_done = [0] * SEGH
            u_done = [0] * SEGH
            gevac = [0] * SEGH
            guevac = [0] * SEGH
            for jj in range(SEGH):
                h = seg * SEGH + jj
                jg, ju = (2 * h) % 6, (2 * h + 1) % 6
                load_stripe(jg, wg_t[h])
                load_stripe(ju, wu_t[h])
                pg = PS_A if jj % 2 == 0 else PS_C
                pu = PS_B if jj % 2 == 0 else PS_D
                wait("tensor", f"ws{jg}", ws_ready[jg])
                wait("tensor", f"ws{ju}", ws_ready[ju])
                if seg == 0 and jj == 0:
                    wait("tensor", "dv", abcopy2)
                if seg == 1 and jj == 0:
                    wait("tensor", "dv", prev_devac15)
                if jj >= 2:
                    wait("tensor", "ac", gevac[jj - 2])
                    wait("tensor", "dv", guevac[jj - 2])
                for n in range(2):
                    for k in range(C):
                        if seg == 0 and jj == 0 and n == 0:
                            wait("tensor", "dv", h2_done[k])
                        g_done[jj] = op(
                            "tensor", "matmul", pg[:, n * NB:(n + 1) * NB],
                            WS[jg][:, k * 128:(k + 1) * 128],
                            H[:, k * T + n * NB:k * T + n * NB + NB],
                            start=(k == 0), stop=(k == C - 1),
                            inc="pe" if (n == 1 and k == C - 1) else None)
                for n in range(2):
                    for k in range(C):
                        u_done[jj] = op(
                            "tensor", "matmul", pu[:, n * NB:(n + 1) * NB],
                            WS[ju][:, k * 128:(k + 1) * 128],
                            H[:, k * T + n * NB:k * T + n * NB + NB],
                            start=(k == 0), stop=(k == C - 1),
                            inc="pe" if (n == 1 and k == C - 1) else None)
                ws_free[jg] = ("pe", u_done[jj])
                ws_free[ju] = ("pe", u_done[jj])

                wait("scalar", "pe", g_done[jj])
                if jj >= 1:
                    wait("scalar", "dv", guevac[jj - 1])
                elif seg == 1:
                    wait("scalar", "dv", prev_guevac[SEGH - 1])
                gevac[jj] = op("scalar", "activation", GTMP, pg[:, :],
                               AF.Silu, bias=0.0,
                               scale=SM[:, SM_SGATE + h:SM_SGATE + h + 1],
                               inc="ac")

                wait("vector", "pe", u_done[jj])
                wait("vector", "ac", gevac[jj])
                guevac[jj] = op("vector", "scalar_tensor_tensor", gut(jj),
                                pu[:, :], SM[:, SM_SUP + h:SM_SUP + h + 1],
                                GTMP, ALU.mult, ALU.mult, inc="dv")

            wait("tensor", "dv", guevac[SEGH - 1])
            csg = 0
            for jj in range(SEGH):
                for n in range(2):
                    csg = op("tensor", "matmul",
                             PS_B[0:1, n * NB:(n + 1) * NB], ONESCB,
                             gut(jj)[:, n * NB:(n + 1) * NB],
                             start=(jj == 0), stop=(jj == SEGH - 1),
                             inc="pe" if (jj == SEGH - 1 and n == 1) else None)
            wait("vector", "pe", csg)
            crowd_done = op("vector", "tensor_scalar", CROWD, PS_B[0:1, :],
                            -128.0, None, ALU.mult, inc="dv")

            down_done = [0] * C
            devac = [0] * C
            for d in range(C):
                j = d % 2
                s, v = wd_free[j]
                wait("gpsimd", s, v)
                wd_ready[j] = dma("gpsimd", WD[j][:, :], wd_t[seg, d],
                                  f"wd{j}")

                pd = PS_A if d % 2 == 0 else PS_C
                wait("tensor", f"wd{j}", wd_ready[j])
                if d == 0:
                    wait("tensor", "dv", crowd_done)
                if d >= 2:
                    wait("tensor", "dv", devac[d - 2])
                for n in range(2):
                    for jj in range(SEGH):
                        op("tensor", "matmul", pd[:, n * NB:(n + 1) * NB],
                           WD[j][:, jj * 128:(jj + 1) * 128],
                           gut(jj)[:, n * NB:(n + 1) * NB],
                           start=(jj == 0), stop=False)
                    down_done[d] = op("tensor", "matmul",
                                      pd[:, n * NB:(n + 1) * NB], ONESRB,
                                      CROWD[0:1, n * NB:(n + 1) * NB],
                                      start=False, stop=True,
                                      inc="pe" if n == 1 else None)
                wd_free[j] = ("pe", down_done[d])

                wait("vector", "pe", down_done[d])
                if seg == 0:
                    devac[d] = op("vector", "scalar_tensor_tensor",
                                  XN[:, d * T:(d + 1) * T], pd[:, :],
                                  SM[:, SM_SDOWN + d:SM_SDOWN + d + 1],
                                  XN[:, d * T:(d + 1) * T], ALU.mult, ALU.add,
                                  inc="dv")
                else:
                    wait("vector", f"so{j}", outb_free[j])
                    devac[d] = op("vector", "scalar_tensor_tensor", OUTB[j],
                                  pd[:, :],
                                  SM[:, SM_SDOWN + d:SM_SDOWN + d + 1],
                                  XN[:, d * T:(d + 1) * T], ALU.mult, ALU.add,
                                  inc="dv")
                    wait("sync", "dv", devac[d])
                    so_cnt[j] = dma("sync", out3[:, d, :], OUTB[j], f"so{j}")
                    outb_free[j] = so_cnt[j]
            prev_gevac = gevac
            prev_guevac = guevac
            prev_devac15 = devac[C - 1]

        wait("sync", "so0", so_cnt[0])
        wait("sync", "so1", so_cnt[1])

        with nc.Block() as block:
            @block.sync
            def _(e):
                for fn in SCHED["sync"]:
                    fn(e)

            @block.gpsimd
            def _(e):
                for fn in SCHED["gpsimd"]:
                    fn(e)

            @block.vector
            def _(e):
                for fn in SCHED["vector"]:
                    fn(e)

            @block.scalar
            def _(e):
                for fn in SCHED["scalar"]:
                    fn(e)

            @block.tensor
            def _(e):
                for fn in SCHED["tensor"]:
                    fn(e)

    return nc


def kernel(**inputs):
    x = np.asarray(inputs["L0_attn_x"], dtype=np.float32)

    def att_w(w):
        a = np.asarray(w, dtype=np.float32).reshape(C, P, C, 128)
        return np.ascontiguousarray(a.transpose(2, 1, 0, 3).reshape(C, P, 2048))

    def ffn_w(w):
        a = np.asarray(w, dtype=np.int32).reshape(HC, 128, C, P)
        return np.ascontiguousarray(a.transpose(0, 3, 2, 1).reshape(HC, P, 2048))

    def down_w(w):
        a = np.asarray(w, dtype=np.int32).reshape(C, 128, NSEG, SEGH, P)
        return np.ascontiguousarray(
            a.transpose(2, 0, 4, 3, 1).reshape(NSEG, C, P, SEGH * 128))

    w_in = att_w(inputs["L0_attn_W_in"])
    w_dec = att_w(inputs["L0_attn_W_decay"])
    w_gate = att_w(inputs["L0_attn_W_gate"])
    w_out = att_w(inputs["L0_attn_W_out"])
    wg_t = ffn_w(inputs["L0_ffn_wq_gate"])
    wu_t = ffn_w(inputs["L0_ffn_wq_up"])
    wd_t = down_w(inputs["L0_ffn_wq_down"])

    def rs(v):
        return np.asarray(v, dtype=np.float32).reshape(-1, 128).T

    smalls = np.ascontiguousarray(np.concatenate(
        [rs(inputs["L0_attn_b_in"]), rs(inputs["L0_attn_b_decay"]),
         rs(inputs["L0_attn_decay_bias"]), rs(inputs["L0_attn_b_gate"]),
         rs(inputs["L0_attn_b_out"]), rs(inputs["L0_ffn_s_down"]),
         rs(inputs["L0_ffn_s_gate"]), rs(inputs["L0_ffn_s_up"])], axis=1))

    xT = np.ascontiguousarray(np.transpose(x, (0, 2, 1)))

    nc = build_program()
    in_maps = []
    for core in range(8):
        b, q = core // 4, core % 4
        qs = np.zeros((P, 4), dtype=np.float32)
        qs[:, q] = 1.0
        in_maps.append(dict(
            xq=np.ascontiguousarray(xT[b][:, q * T:(q + 1) * T]),
            w_in=w_in, w_dec=w_dec, w_gate=w_gate, w_out=w_out,
            wg_t=wg_t, wu_t=wu_t, wd_t=wd_t, smalls=smalls, qsel=qs))

    res = run_bass_kernel_spmd(nc, in_maps, core_ids=list(range(8)))

    outT = np.empty((2, D, 4096), dtype=np.float32)
    for core in range(8):
        b, q = core // 4, core % 4
        outT[b][:, q * T:(q + 1) * T] = np.asarray(res.results[core]["outT"])
    return np.ascontiguousarray(np.transpose(outT, (0, 2, 1)))
